# version 03 (final)
# baseline (speedup 1.0000x reference)
import math
from contextlib import ExitStack
import numpy as np
import ml_dtypes
import concourse.bass as bass
import concourse.mybir as mybir
from concourse.bass import IndirectOffsetOnAxis
from concourse.bass_utils import run_bass_kernel_spmd

F32 = mybir.dt.float32
BF16 = mybir.dt.bfloat16
U32 = mybir.dt.uint32
AF = mybir.ActivationFunctionType
ALU = mybir.AluOpType
AXX = mybir.AxisListType.X

MAGIC = 12582912.0
D = 2048
L = 2048
NF = 2176
NFT = 17
ISQ = 1.0 / math.sqrt(128.0)
bfdt = ml_dtypes.bfloat16


def _host_constants():
    c = {}
    n = 2 * L
    t_idx = np.arange(L, dtype=np.float64)
    f_idx = np.arange(NF, dtype=np.float64)
    ang = 2.0 * np.pi * np.outer(t_idx, f_idx) / n
    Fc = np.cos(ang); Fs = -np.sin(ang)
    Fc[:, 2049:] = 0.0; Fs[:, 2049:] = 0.0
    wf = np.full(NF, 2.0); wf[0] = 1.0; wf[2048] = 1.0; wf[2049:] = 0.0
    angi = 2.0 * np.pi * np.outer(f_idx, t_idx) / n
    Gc = (wf[:, None] / n) * np.cos(angi)
    Gs = -(wf[:, None] / n) * np.sin(angi)
    c["Fc"] = np.ascontiguousarray(Fc.astype(np.float32).astype(bfdt))
    c["Fs"] = np.ascontiguousarray(Fs.astype(np.float32).astype(bfdt))
    c["Gc"] = np.ascontiguousarray(Gc.astype(np.float32).astype(bfdt))
    c["Gs"] = np.ascontiguousarray(Gs.astype(np.float32).astype(bfdt))

    tt = np.linspace(0.0, 1.0, L)[:, None]
    t_resc = np.linspace(0.0, L - 1.0, L)
    w = 2.0 * np.pi * t_resc / L
    angz = 0.0001 * w[:, None]
    z = np.concatenate([tt, np.cos(angz), -np.sin(angz)], axis=-1)
    c["zT3"] = np.ascontiguousarray(z.T.astype(np.float32))
    c["tvec"] = np.ascontiguousarray(tt.T.astype(np.float32))

    Dh = 128
    pos = np.arange(L, dtype=np.float32)[:, None]
    i2 = np.arange(0, Dh, 2, dtype=np.float32)
    inv_freq = np.exp(-(np.log(10000.0) * i2 / Dh)).astype(np.float32)
    angp = pos * inv_freq
    pe = np.zeros((L, Dh), dtype=np.float32)
    pe[:, 0::2] = np.sin(angp); pe[:, 1::2] = np.cos(angp)
    c["peT"] = np.ascontiguousarray(pe.T)

    masks = np.zeros((4, 128, 512), dtype=np.float32)
    kl = np.arange(128)[:, None]; ql = np.arange(512)[None, :]
    for j in range(4):
        masks[j] = (128 * j + kl <= ql).astype(np.float32)
    c["masks"] = np.ascontiguousarray(masks.reshape(4, 128, 512).transpose(1, 0, 2)
                                      .reshape(128, 4 * 512).astype(bfdt))
    c["ones_bf"] = np.ones((128, 1), dtype=bfdt)
    c["ones1f"] = np.ones((1, 128), dtype=np.float32)
    c["ones1b"] = np.ones((1, 128), dtype=bfdt)
    c["ident"] = np.ascontiguousarray(np.eye(128, dtype=bfdt))
    c["identf"] = np.ascontiguousarray(np.eye(128, dtype=np.float32))
    return c


def _per_core_inputs(core, inp, consts):
    b, h = core // 2, core % 2
    P_my = np.arange(h * 1024, (h + 1) * 1024)
    sigma = np.concatenate([P_my, np.arange((1 - h) * 1024, (2 - h) * 1024)])
    rows = slice(h * 1024, (h + 1) * 1024)

    m = dict(consts)
    x = np.asarray(inp["L0_attn_x"][b], dtype=np.float32)
    m["xT"] = np.ascontiguousarray(x.T)
    m["xoT"] = np.ascontiguousarray(x.T[:, rows])

    W = np.asarray(inp["L0_attn_in_proj_w"], dtype=np.float32)
    ipW = np.concatenate([W[P_my], W[2048 + P_my], W[4096 + P_my]], axis=0)
    m["ipWT"] = np.ascontiguousarray(ipW.T)
    bvec = np.asarray(inp["L0_attn_in_proj_b"], dtype=np.float32)
    ipb = np.stack([bvec[P_my], bvec[2048 + P_my], bvec[4096 + P_my]])
    m["ipB"] = np.ascontiguousarray(ipb.reshape(3, 8, 128).transpose(2, 0, 1).reshape(128, 24))
    sw = np.asarray(inp["L0_attn_short_w"], dtype=np.float32)[:, 0, :]
    swm = np.stack([sw[P_my], sw[2048 + P_my], sw[4096 + P_my]])
    m["swT"] = np.ascontiguousarray(swm.reshape(3, 8, 128, 3).transpose(2, 0, 1, 3).reshape(128, 72))
    sb = np.asarray(inp["L0_attn_short_b"], dtype=np.float32)
    sbm = np.stack([sb[P_my], sb[2048 + P_my], sb[4096 + P_my]])
    m["sbT"] = np.ascontiguousarray(sbm.reshape(3, 8, 128).transpose(2, 0, 1).reshape(128, 24))

    m["w0T"] = np.ascontiguousarray(np.asarray(inp["L0_attn_mlp_w0"], np.float32).T)
    m["w1T"] = np.ascontiguousarray(np.asarray(inp["L0_attn_mlp_w1"], np.float32).T)
    m["w2T"] = np.ascontiguousarray(np.asarray(inp["L0_attn_mlp_w2"], np.float32).T)
    fsml = np.zeros((64, 4), dtype=np.float32)
    fsml[:, 0] = np.asarray(inp["L0_attn_mlp_b0"], np.float32)
    fsml[:, 1] = np.asarray(inp["L0_attn_mlp_b1"], np.float32)
    fsml[:, 2] = np.asarray(inp["L0_attn_mlp_b2"], np.float32)
    fsml[:, 3] = np.asarray(inp["L0_attn_sin_freq"], np.float32).reshape(64)
    m["fsml"] = np.ascontiguousarray(fsml)
    m["w3T"] = np.ascontiguousarray(np.asarray(inp["L0_attn_mlp_w3"], np.float32).T[:, P_my])
    max_decay = math.log(0.01) / 0.3
    min_decay = math.log(0.01) / 1.5
    deltas = np.linspace(min_decay, max_decay, D).astype(np.float32)
    m["dabs"] = np.ascontiguousarray(np.abs(deltas)[None, P_my])
    m["fbT"] = np.ascontiguousarray(np.asarray(inp["L0_attn_filter_bias"], np.float32)[P_my]
                                    .reshape(8, 128).T)
    Wo = np.asarray(inp["L0_attn_out_proj_w"], np.float32)
    m["opWT"] = np.ascontiguousarray(Wo[:, sigma].T)
    m["opB"] = np.ascontiguousarray(np.asarray(inp["L0_attn_out_proj_b"], np.float32)
                                    .reshape(16, 128).T)

    for lyr, pre in ((0, "L0"), (1, "L1")):
        g = np.asarray(inp[f"{pre}_ffn_w_gate"], np.float32)
        u = np.asarray(inp[f"{pre}_ffn_w_up"], np.float32)
        d = np.asarray(inp[f"{pre}_ffn_w_down"], np.float32)
        m[f"gwT{lyr}"] = np.ascontiguousarray(g.T)
        m[f"uwT{lyr}"] = np.ascontiguousarray(u.T)
        m[f"dwT{lyr}"] = np.ascontiguousarray(d.T)
        m[f"gwh{lyr}"] = np.ascontiguousarray(g.T[:, h * 4096:(h + 1) * 4096])
        m[f"uwh{lyr}"] = np.ascontiguousarray(u.T[:, h * 4096:(h + 1) * 4096])
        m[f"dwh{lyr}"] = np.ascontiguousarray(d.T[:, h * 1024:(h + 1) * 1024])

    wq = np.asarray(inp["L1_attn_wq"], np.float32)
    bq = np.asarray(inp["L1_attn_bq"], np.float32)
    wkv = np.asarray(inp["L1_attn_wkv"], np.float32)
    bkv = np.asarray(inp["L1_attn_bkv"], np.float32)
    wo1 = np.asarray(inp["L1_attn_wo"], np.float32)
    hc = slice(1024 * h, 1024 * (h + 1))
    m["wqm"] = np.ascontiguousarray(wq[:, hc])
    m["bqm"] = np.ascontiguousarray(bq[hc].reshape(8, 128).T)
    gsel = [2 * h, 2 * h + 1]
    kcols = np.concatenate([np.arange(g * 256, g * 256 + 128) for g in gsel])
    vcols = np.concatenate([np.arange(g * 256 + 128, g * 256 + 256) for g in gsel])
    m["wkm"] = np.ascontiguousarray(wkv[:, kcols])
    m["wvm"] = np.ascontiguousarray(wkv[:, vcols])
    m["bkm"] = np.ascontiguousarray(bkv[kcols].reshape(2, 128).T)
    m["bvm"] = np.ascontiguousarray(bkv[vcols].reshape(2, 128).T)
    m["woP"] = np.ascontiguousarray(
        np.concatenate([wo1[hc], wo1[slice(1024 * (1 - h), 1024 * (2 - h))]], axis=0))

    p = np.arange(128, dtype=np.uint32)
    m["idxs"] = np.ascontiguousarray((h * 128 + p)[:, None])
    m["idxw"] = np.ascontiguousarray(np.array([[h]], dtype=np.uint32))
    g16 = np.zeros((128, 16), dtype=np.uint32)
    for j in range(16):
        slot = h if j < 8 else 1 - h
        g16[:, j] = slot * 2048 + p * 16 + (j % 8) * 2 + h
    m["idxg16"] = np.ascontiguousarray(g16)
    m["idxg16T"] = np.ascontiguousarray(g16.T)
    return m


class NS:
    pass


class Prog:
    def __init__(self):
        self.nc = bass.Bass("TRN2", target_bir_lowering=False, num_devices=8)
        self.sems = {}
        self.cnt = {}
        self.uid = 0
        for s in ["LD", "LS", "LG", "LV", "T", "V", "S", "G", "ST"]:
            self.sems[s] = self.nc.alloc_semaphore(name=s)
            self.cnt[s] = 0

    def sb(self, name, shape, dt):
        self.uid += 1
        return self.nc.sbuf_tensor("%s_u%d" % (name, self.uid), shape, dt)

    def ps(self, name, shape, dt):
        self.uid += 1
        return self.nc.psum_tensor("%s_u%d" % (name, self.uid), shape, dt)

    def inc(self, name, ins, n=1):
        ins.then_inc(self.sems[name], n)
        self.cnt[name] += n
        return ins

    def w(self, eng, name):
        eng.wait_ge(self.sems[name], self.cnt[name])

    def dma(self, eng, sem, out, in_):
        self.inc(sem, eng.dma_start(out=out, in_=in_), 16)

    def vbounce(self):
        nc = self.nc
        self.w(nc.scalar, "V")
        nc.scalar.sem_inc(self.sems["S"], 1)
        self.cnt["S"] += 1
        self.w(nc.vector, "S")

    def bar(self):
        nc = self.nc
        engines = [nc.sync, nc.scalar, nc.vector, nc.tensor, nc.gpsimd]
        base = self.cnt["G"]
        for eng in engines:
            for s in ("LD", "LS", "LG", "LV", "ST"):
                if self.cnt[s]:
                    eng.wait_ge(self.sems[s], self.cnt[s])
            eng.sem_inc(self.sems["G"], 1)
            self.cnt["G"] += 1
        for eng in engines:
            eng.wait_ge(self.sems["G"], base + 5)

    def core_bar(self):
        self.bar()
        self.nc.all_core_barrier()
        self.bar()


def build(debug=()):
    P = Prog()
    nc = P.nc
    P.debug = set(debug)
    e = NS()
    P.e = e

    def din(name, shape, dt):
        setattr(e, name, nc.dram_tensor(name, list(shape), dt, kind="ExternalInput").ap())
    din("xT", (D, L), F32); din("xoT", (D, 1024), F32)
    din("ipWT", (D, 3072), F32); din("ipB", (128, 24), F32)
    din("swT", (128, 72), F32); din("sbT", (128, 24), F32)
    din("w0T", (3, 64), F32); din("w1T", (64, 64), F32); din("w2T", (64, 64), F32)
    din("fsml", (64, 4), F32); din("w3T", (64, 1024), F32)
    din("dabs", (1, 1024), F32); din("fbT", (128, 8), F32)
    din("opWT", (D, D), F32); din("opB", (128, 16), F32)
    din("Fc", (L, NF), BF16); din("Fs", (L, NF), BF16)
    din("Gc", (NF, L), BF16); din("Gs", (NF, L), BF16)
    din("zT3", (3, L), F32); din("tvec", (1, L), F32)
    din("peT", (128, L), F32); din("masks", (128, 4 * 512), BF16)
    din("ones_bf", (128, 1), BF16); din("ones1f", (1, 128), F32); din("ones1b", (1, 128), BF16)
    din("ident", (128, 128), BF16); din("identf", (128, 128), F32)
    for lyr in (0, 1):
        din(f"gwT{lyr}", (D, 8192), F32); din(f"uwT{lyr}", (D, 8192), F32)
        din(f"dwT{lyr}", (8192, D), F32)
        din(f"gwh{lyr}", (D, 4096), F32); din(f"uwh{lyr}", (D, 4096), F32)
        din(f"dwh{lyr}", (8192, 1024), F32)
    din("wqm", (D, 1024), F32); din("bqm", (128, 8), F32)
    din("wkm", (D, 256), F32); din("wvm", (D, 256), F32)
    din("bkm", (128, 2), F32); din("bvm", (128, 2), F32)
    din("woP", (D, D), F32)
    din("idxs", (128, 1), U32); din("idxw", (1, 1), U32); din("idxg16", (128, 16), U32); din("idxg16T", (16, 128), U32)

    e.outT = nc.dram_tensor("outT", [D, 1024], F32, kind="ExternalOutput").ap()

    def scr(name, shape, dt):
        setattr(e, name, nc.dram_tensor(name, list(shape), dt, kind="Internal").ap())
    scr("sc_x0", (1024, L), BF16); scr("sc_vb", (1024, L), BF16)
    scr("sc_xq0", (D, 1024), BF16); scr("sc_xq1", (D, 1024), BF16)
    scr("sc_s0", (D, 1024), F32); scr("sc_s1", (D, 1024), F32)
    scr("sc_x1", (D, 1024), F32)
    scr("sc_h", (8192, 1024), F32)
    scr("sc_dw", (8192, D), BF16)

    def shr(name, shape, dt):
        setattr(e, name, nc.dram_tensor(name, list(shape), dt, kind="Internal",
                                        addr_space="Shared").ap())
    shr("sh_y", (256, 8 * L), BF16)
    shr("sh_x1", (256, 16 * 1024), F32)
    shr("sh_at", (256, 8 * L), BF16)
    shr("sh_ws", (256, 8), F32)

    P.dbg_bufs = {}

    def dump(name, ap, shape, dt):
        if name in P.debug:
            o = nc.dram_tensor("dbg_" + name, list(shape), dt, kind="ExternalOutput").ap()
            P.w(nc.scalar, "V"); P.w(nc.scalar, "S"); P.w(nc.scalar, "T")
            P.dma(nc.scalar, "LS", o, ap)
            P.w(nc.scalar, "LS")
    P.dump = dump

    def dump_dram(name, dram_ap, shape, dt):
        if name not in P.debug:
            return
        rows, cols = shape
        o = nc.dram_tensor("dbg_" + name, [rows, cols], dt, kind="ExternalOutput").ap()
        with P.sb("dmp", [128, cols], dt) as tile:
            for r in range(rows // 128):
                P.w(nc.sync, "LS")
                P.dma(nc.sync, "LD", tile[:], dram_ap[r * 128:(r + 1) * 128, :])
                P.w(nc.scalar, "LD")
                P.dma(nc.scalar, "LS", o[r * 128:(r + 1) * 128, :], tile[:])
        P.w(nc.scalar, "LS")
    P.dump_dram = dump_dram

    _emit(P)
    return nc


def _emit(P):
    nc, e = P.nc, P.e
    names = [
        ("onesb", [128, 1], BF16), ("ones1f", [1, 128], F32), ("ones1b", [1, 128], BF16),
        ("ident", [128, 128], BF16), ("identf", [128, 128], F32),
        ("ipB", [128, 24], F32), ("swT", [128, 72], F32),
        ("sbT", [128, 24], F32), ("fbT", [128, 8], F32), ("opB", [128, 16], F32),
        ("bqm", [128, 8], F32), ("bkm", [128, 2], F32), ("bvm", [128, 2], F32),
        ("idxs", [128, 1], U32), ("idxw", [1, 1], U32), ("idxg", [128, 16], U32),
        ("fsml", [64, 4], F32),
        ("pb", [128, 16], F32), ("rsx", [1, 2 * 1024], F32),
        ("nmb", [128, 32], F32), ("swv", [1, 16], F32), ("magic", [128, 1], F32),
    ] + [("idxg%d" % j, [128, 1], U32) for j in range(16)]
    with ExitStack() as es:
        B = NS()
        P.B = B
        for nm, shp, dt in names:
            setattr(B, nm, es.enter_context(P.sb("c_" + nm, shp, dt)))

        _load_consts(P)
        _wsum_pass(P)
        _own_quant(P, 0)
        P.dump_dram("xq0a", e.sc_xq0, (2048, 1024), BF16)
        P.core_bar()
        _wsum_combine(P)

        with P.sb("VSM", [128, 16 * 1024], BF16) as VSM:
            with P.sb("XN", [128, 16 * L], BF16) as XN:
                _full_xn(P, XN, src="x0")
                P.dump("xn", XN[:], (128, 16 * L), BF16)
                _l0_inproj(P, XN, VSM)
            P.dump_dram("xq0b", e.sc_xq0, (2048, 1024), BF16)
            with P.sb("KSM", [128, 16 * 1024], BF16) as KSM:
                _l0_filter(P, KSM)
                _l0_dft(P, VSM, KSM, VSM)
                P.dump_dram("xq0c", e.sc_xq0, (2048, 1024), BF16)
                _l0_gate_exchange(P, VSM, KSM, KSM)
                P.dump_dram("xq0d", e.sc_xq0, (2048, 1024), BF16)
                _l0_outproj(P, KSM)
        P.dump_dram("xq0", e.sc_xq0, (2048, 1024), BF16)
        P.dump_dram("s0", e.sc_s0, (2048, 1024), F32)
        P.dump_dram("x0g", e.sc_x0, (1024, 2048), BF16)
        P.dump_dram("vb", e.sc_vb, (1024, 2048), BF16)
        _ffn(P, 0)
        P.dump_dram("x1", e.sc_x1, (2048, 1024), F32)
        _e1_scatter(P)
        _own_quant(P, 1)
        with ExitStack() as es2:
            B.peT = es2.enter_context(P.sb("c_peT", [128, L], F32))
            B.masks = es2.enter_context(P.sb("c_masks", [128, 4 * 512], BF16))
            P.dma(nc.sync, "LD", B.peT[:], e.peT)
            P.dma(nc.sync, "LD", B.masks[:], e.masks)
            with (
                P.sb("Q", [128, 8 * L], BF16) as Q,
                P.sb("KT", [128, 2 * L], BF16) as KT,
                P.sb("VSM2", [128, 16 * 256], BF16) as VSM2,
            ):
                with P.sb("XN1", [128, 16 * L], BF16) as XN1:
                    _full_xn(P, XN1, src="x1")
                    _l1_qkv(P, XN1, Q, KT, VSM2)
                P.dump("q1", Q[:], (128, 8 * L), BF16)
                P.dump("k1", KT[:], (128, 2 * L), BF16)
                with P.sb("ATTN", [128, 8 * L], BF16) as ATTN:
                    _l1_attn(P, Q, KT, VSM2, ATTN)
                    P.dump("attn", ATTN[:], (128, 8 * L), BF16)
                    _e2_exchange(P, ATTN)
            with P.sb("AASM", [128, 16 * 1024], BF16) as AASM:
                _e2_gather_outproj(P, AASM)
        P.dump_dram("s1", e.sc_s1, (2048, 1024), F32)
        _ffn(P, 1)
        _finish(P)


def _load_consts(P):
    nc, e, B = P.nc, P.e, P.B
    pairs = [
        (B.onesb[:], e.ones_bf), (B.ones1f[:], e.ones1f), (B.ones1b[:], e.ones1b),
        (B.ident[:], e.ident),
        (B.ipB[:], e.ipB), (B.swT[:], e.swT), (B.sbT[:], e.sbT), (B.fbT[:], e.fbT),
        (B.opB[:], e.opB), (B.bqm[:], e.bqm), (B.bkm[:], e.bkm), (B.bvm[:], e.bvm),
        (B.identf[:], e.identf),
        (B.idxs[:], e.idxs), (B.idxw[:], e.idxw), (B.idxg[:], e.idxg16),
        (B.fsml[:], e.fsml),
    ]
    for j in range(16):
        pairs.append((getattr(B, "idxg%d" % j)[:], e.idxg16T[j:j + 1, :].rearrange("a p -> (a p)")[:, None]))
    for dst, src in pairs:
        P.dma(nc.sync, "LD", dst, src)
    P.inc("V", nc.vector.memset(B.magic[:], MAGIC))
    P.w(nc.sync, "LD")
    P.bar()


def _wsum_pass(P):
    nc, e, B = P.nc, P.e, P.B
    full = [e.gwT0, e.uwT0, e.dwT0, e.gwT1, e.uwT1, e.dwT1]
    with (
        P.sb("ws_ld", [128, 2 * 4096], F32) as wld,
        P.sb("ws_ab", [128, 4096], BF16) as wab,
        P.ps("ws_ps", [1, 512], F32) as ps,
    ):
        for wi, wap in enumerate(full):
            rows_total, cols = wap.shape
            csplit = 2 if cols > 4096 else 1
            cw = cols // csplit
            nt = rows_total // 128
            nchunk = cw // 512
            nmm = 0
            total_mm = nt * csplit * nchunk
            for t in range(nt):
                for cs in range(csplit):
                    bp = (t * csplit + cs) % 2
                    if nmm >= 2 * nchunk:
                        P.w(nc.sync, "V")
                    P.dma(nc.sync, "LD", wld[:, bp * 4096:bp * 4096 + cw],
                          wap[t * 128:(t + 1) * 128, cs * cw:(cs + 1) * cw])
                    P.w(nc.vector, "LD")
                    P.w(nc.vector, "T")
                    s2 = wld[:, bp * 4096:bp * 4096 + cw]
                    P.inc("V", nc.vector.scalar_tensor_tensor(
                        out=wab[:, :cw], in0=s2, scalar=-1.0, in1=s2,
                        op0=ALU.mult, op1=ALU.max))
                    P.w(nc.tensor, "V")
                    for ch in range(nchunk):
                        P.inc("T", nc.tensor.matmul(
                            ps[:], B.onesb[:], wab[:, ch * 512:(ch + 1) * 512],
                            start=(nmm == 0), stop=(nmm == total_mm - 1)))
                        nmm += 1
            P.w(nc.vector, "T")
            P.inc("V", nc.vector.tensor_reduce(
                out=B.swv[0:1, 8 + wi:9 + wi], in_=ps[:], axis=AXX, op=ALU.add))
            P.vbounce()
            P.inc("V", nc.vector.tensor_scalar(
                out=B.swv[0:1, wi:wi + 1], in0=B.swv[0:1, 8 + wi:9 + wi],
                scalar1=1.0 / float(8192 * 2048), scalar2=1e-5,
                op0=ALU.mult, op1=(ALU.max if wi < 3 else ALU.add)))
    P.bar()


def _wsum_combine(P):
    nc, e, B = P.nc, P.e, P.B
    sv = B.swv
    P.inc("V", nc.vector.reciprocal(out=sv[:, 8:14], in_=sv[:, 0:6]))
    P.vbounce()
    with P.ps("wc_ps", [128, 16], F32) as psb:
        P.w(nc.tensor, "V")
        P.inc("T", nc.tensor.matmul(psb[:], B.ones1f[:], sv[:, 0:16], start=True, stop=True))
        P.w(nc.vector, "T")
        P.inc("V", nc.vector.tensor_copy(out=B.pb[:, 0:6], in_=psb[:, 8:14]))
        P.inc("V", nc.vector.tensor_copy(out=B.pb[:, 6:12], in_=psb[:, 0:6]))
        P.dump("swv", B.swv[:], (1, 16), F32)
        P.dump("pb", B.pb[:], (128, 16), F32)
    P.bar()


def _rstd_from_var(P, dst, var, tmp1, tmp2):
    nc = P.nc
    P.w(nc.scalar, "V")
    P.inc("S", nc.scalar.sqrt(out=tmp1, in_=var))
    P.w(nc.vector, "S")
    P.inc("V", nc.vector.reciprocal(out=tmp2, in_=tmp1))
    P.vbounce()
    P.inc("V", nc.vector.tensor_tensor(out=tmp2, in0=var, in1=tmp2, op=ALU.mult))
    P.inc("V", nc.vector.tensor_tensor(out=tmp1, in0=tmp1, in1=tmp2, op=ALU.add))
    P.inc("V", nc.vector.tensor_scalar(out=tmp1, in0=tmp1, scalar1=0.5, scalar2=None, op0=ALU.mult))
    P.inc("V", nc.vector.reciprocal(out=dst, in_=tmp1))
    P.vbounce()


def _bcast(P, src_row, dst, n):
    nc, B = P.nc, P.B
    with P.ps("bc_ps", [128, 512], F32) as ps:
        nchunk = max(1, n // 512)
        for c in range(nchunk):
            w = min(512, n)
            sl = slice(c * 512, c * 512 + w)
            P.w(nc.tensor, "V")
            P.inc("T", nc.tensor.matmul(ps[:, 0:w], B.ones1f[:], src_row[:, sl], start=True, stop=True))
            P.w(nc.vector, "T")
            P.inc("V", nc.vector.tensor_copy(out=dst[:, sl], in_=ps[:, 0:w]))


def _pmax_rows(P, src128, ncols, rowmax, rowmin, colbuf, psT, psR):
    nc, B = P.nc, P.B
    for t in range(ncols // 128):
        tile = src128[:, t * 128:(t + 1) * 128]
        P.w(nc.tensor, "V")
        P.inc("T", nc.tensor.matmul(psT[:], tile, B.identf[:], start=True, stop=True))
        P.w(nc.vector, "T")
        P.inc("V", nc.vector.tensor_reduce(out=colbuf[:, 0:1], in_=psT[:], axis=AXX, op=ALU.max))
        P.inc("V", nc.vector.tensor_reduce(out=colbuf[:, 1:2], in_=psT[:], axis=AXX, op=ALU.min))
        for j, rowdst in ((0, rowmax), (1, rowmin)):
            P.w(nc.tensor, "V")
            P.inc("T", nc.tensor.matmul(psR[:], colbuf[:, j:j + 1], B.identf[:], start=True, stop=True))
            P.w(nc.vector, "T")
            P.inc("V", nc.vector.tensor_copy(out=rowdst[:, t * 128:(t + 1) * 128], in_=psR[:]))


def _full_xn(P, XN, src):
    nc, e, B = P.nc, P.e, P.B
    with (
        P.sb("fx_ld", [128, L], F32) as ld,
        P.sb("fx_bf", [128, L], BF16) as xb,
        P.sb("fx_sq", [128, L], BF16) as xs,
        P.sb("fx_bc", [128, 2 * L], F32) as bc,
        P.sb("fx_tmp", [128, L], F32) as tmpf,
        P.sb("fx_row", [1, 2 * L], F32) as row,
    ):
        def load(kt):
            if src == "x0":
                P.dma(nc.sync, "LD", ld[:], e.xT[kt * 128:(kt + 1) * 128, :])
            else:
                for s in range(2):
                    P.dma(nc.sync, "LD", ld[:, s * 1024:(s + 1) * 1024],
                          e.sh_x1[s * 128:(s + 1) * 128, kt * 1024:(kt + 1) * 1024])

        with ExitStack() as es:
            psum_s = [es.enter_context(P.ps("fx_p%d" % i, [1, 512], F32)) for i in range(4)]
            psum_q = [es.enter_context(P.ps("fx_q%d" % i, [1, 512], F32)) for i in range(4)]
            for kt in range(16):
                if kt >= 1:
                    P.w(nc.sync, "V")
                load(kt)
                P.w(nc.vector, "LD")
                P.w(nc.vector, "T")
                P.inc("V", nc.vector.tensor_copy(out=xb[:], in_=ld[:]))
                P.inc("V", nc.vector.tensor_tensor(out=xs[:], in0=xb[:], in1=xb[:], op=ALU.mult))
                P.w(nc.tensor, "V")
                for ch in range(4):
                    sl = slice(ch * 512, (ch + 1) * 512)
                    P.inc("T", nc.tensor.matmul(psum_s[ch][:], B.onesb[:], xb[:, sl],
                                                start=(kt == 0), stop=(kt == 15)))
                    P.inc("T", nc.tensor.matmul(psum_q[ch][:], B.onesb[:], xs[:, sl],
                                                start=(kt == 0), stop=(kt == 15)))
            P.w(nc.vector, "T")
            for ch in range(4):
                sl = slice(ch * 512, (ch + 1) * 512)
                sl2 = slice(L + ch * 512, L + (ch + 1) * 512)
                P.inc("V", nc.vector.tensor_scalar(out=row[:, sl], in0=psum_s[ch][:],
                                                   scalar1=1.0 / D, scalar2=None, op0=ALU.mult))
                P.inc("V", nc.vector.tensor_scalar(out=row[:, sl2], in0=psum_q[ch][:],
                                                   scalar1=1.0 / D, scalar2=None, op0=ALU.mult))
        _bcast(P, row[:, 0:L], bc[:, 0:L], L)
        _bcast(P, row[:, L:2 * L], bc[:, L:2 * L], L)
        P.inc("V", nc.vector.tensor_tensor(out=tmpf[:], in0=bc[:, 0:L], in1=bc[:, 0:L], op=ALU.mult))
        P.inc("V", nc.vector.tensor_tensor(out=bc[:, L:2 * L], in0=bc[:, L:2 * L], in1=tmpf[:], op=ALU.subtract))
        P.inc("V", nc.vector.tensor_scalar(out=bc[:, L:2 * L], in0=bc[:, L:2 * L], scalar1=1e-5,
                                           scalar2=None, op0=ALU.add))
        _rstd_from_var(P, bc[:, L:2 * L], bc[:, L:2 * L], tmpf[:], ld[:])
        for kt in range(16):
            P.w(nc.sync, "V")
            load(kt)
            P.w(nc.vector, "LD")
            P.inc("V", nc.vector.tensor_tensor(out=tmpf[:], in0=ld[:], in1=bc[:, 0:L], op=ALU.subtract))
            P.inc("V", nc.vector.tensor_tensor(out=XN[:, kt * L:(kt + 1) * L], in0=tmpf[:],
                                               in1=bc[:, L:2 * L], op=ALU.mult))
    P.bar()


def _own_quant(P, layer):
    nc, e, B = P.nc, P.e, P.B
    src = e.xoT if layer == 0 else e.sc_x1
    n = 1024
    with (
        P.sb("oq_x", [128, 16 * 1024], F32) as ox,
        P.sb("oq_bf", [128, 1024], BF16) as ob,
        P.sb("oq_sq", [128, 1024], BF16) as osq,
        P.sb("oq_mx", [128, 1024], F32) as omx,
        P.sb("oq_mn", [128, 1024], F32) as omn,
        P.sb("oq_ab", [128, 2 * 1024], F32) as oab,
        P.sb("oq_t", [128, 1024], F32) as ot,
        P.sb("oq_q", [128, 1024], BF16) as oq,
        P.sb("oq_r", [1, 8 * 1024], F32) as R,
        P.sb("oq_cb", [128, 2], F32) as colbuf,
        P.ps("oq_p0", [1, 512], F32) as p0,
        P.ps("oq_p1", [1, 512], F32) as p1,
        P.ps("oq_q0", [1, 512], F32) as q0,
        P.ps("oq_q1", [1, 512], F32) as q1,
        P.ps("oq_pT", [128, 128], F32) as psT,
        P.ps("oq_pR", [1, 128], F32) as psR,
    ):
        def row(i):
            return R[:, i * n:(i + 1) * n]
        for kt in range(16):
            P.dma(nc.sync, "LD", ox[:, kt * 1024:(kt + 1) * 1024], src[kt * 128:(kt + 1) * 128, :])
        P.w(nc.vector, "LD")
        for kt in range(16):
            x = ox[:, kt * 1024:(kt + 1) * 1024]
            P.w(nc.vector, "T")
            P.inc("V", nc.vector.tensor_copy(out=ob[:], in_=x))
            P.inc("V", nc.vector.tensor_tensor(out=osq[:], in0=ob[:], in1=ob[:], op=ALU.mult))
            if kt == 0:
                P.inc("V", nc.vector.tensor_copy(out=omx[:], in_=x))
                P.inc("V", nc.vector.tensor_copy(out=omn[:], in_=x))
            else:
                P.inc("V", nc.vector.tensor_tensor(out=omx[:], in0=omx[:], in1=x, op=ALU.max))
                P.inc("V", nc.vector.tensor_tensor(out=omn[:], in0=omn[:], in1=x, op=ALU.min))
            P.w(nc.tensor, "V")
            for ch, (ps, pq) in enumerate([(p0, q0), (p1, q1)]):
                sl = slice(ch * 512, (ch + 1) * 512)
                P.inc("T", nc.tensor.matmul(ps[:], B.onesb[:], ob[:, sl], start=(kt == 0), stop=(kt == 15)))
                P.inc("T", nc.tensor.matmul(pq[:], B.onesb[:], osq[:, sl], start=(kt == 0), stop=(kt == 15)))
        P.w(nc.vector, "T")
        for ch, (ps, pq) in enumerate([(p0, q0), (p1, q1)]):
            sl = slice(ch * 512, (ch + 1) * 512)
            P.inc("V", nc.vector.tensor_scalar(out=row(0)[:, sl], in0=ps[:], scalar1=1.0 / D,
                                               scalar2=None, op0=ALU.mult))
            P.inc("V", nc.vector.tensor_scalar(out=row(1)[:, sl], in0=pq[:], scalar1=1.0 / D,
                                               scalar2=None, op0=ALU.mult))
        P.inc("V", nc.vector.tensor_tensor(out=row(6), in0=row(0), in1=row(0), op=ALU.mult))
        P.inc("V", nc.vector.tensor_tensor(out=row(1), in0=row(1), in1=row(6), op=ALU.subtract))
        P.inc("V", nc.vector.tensor_scalar(out=row(1), in0=row(1), scalar1=1e-5, scalar2=None, op0=ALU.add))
        _rstd_from_var(P, row(1), row(1), row(6), row(7))
        _pmax_rows(P, omx, 1024, row(2), row(6), colbuf, psT, psR)
        _pmax_rows(P, omn, 1024, row(7), row(3), colbuf, psT, psR)
        P.inc("V", nc.vector.tensor_tensor(out=row(2), in0=row(2), in1=row(0), op=ALU.subtract))
        P.inc("V", nc.vector.tensor_tensor(out=row(3), in0=row(0), in1=row(3), op=ALU.subtract))
        P.inc("V", nc.vector.tensor_tensor(out=row(2), in0=row(2), in1=row(3), op=ALU.max))
        P.inc("V", nc.vector.tensor_tensor(out=row(2), in0=row(2), in1=row(1), op=ALU.mult))
        P.inc("V", nc.vector.tensor_scalar(out=row(2), in0=row(2), scalar1=1e-5, scalar2=None, op0=ALU.max))
        P.inc("V", nc.vector.tensor_scalar(out=B.rsx[0:1, layer * 1024:(layer + 1) * 1024],
                                           in0=row(2), scalar1=1.0 / 127.0, scalar2=None, op0=ALU.mult))
        P.inc("V", nc.vector.reciprocal(out=row(3), in_=row(2)))
        P.vbounce()
        P.inc("V", nc.vector.tensor_scalar(out=row(3), in0=row(3), scalar1=127.0, scalar2=None, op0=ALU.mult))
        P.inc("V", nc.vector.tensor_tensor(out=row(4), in0=row(1), in1=row(3), op=ALU.mult))
        P.inc("V", nc.vector.tensor_tensor(out=row(5), in0=row(0), in1=row(4), op=ALU.mult))
        P.inc("V", nc.vector.tensor_scalar(out=row(5), in0=row(5), scalar1=-1.0, scalar2=None, op0=ALU.mult))
        if layer == 0:
            P.dump("oqr", R[:], (1, 8 * 1024), F32)
        _bcast(P, row(4), oab[:, 0:1024], 1024)
        _bcast(P, row(5), oab[:, 1024:2048], 1024)
        if layer == 0:
            P.dump("oab", oab[:], (128, 2048), F32)
        dst = e.sc_xq0 if layer == 0 else e.sc_xq1
        for kt in range(16):
            x = ox[:, kt * 1024:(kt + 1) * 1024]
            if kt > 0:
                P.w(nc.vector, "LS")
            P.inc("V", nc.vector.tensor_tensor(out=ot[:], in0=x, in1=oab[:, 0:1024], op=ALU.mult))
            P.inc("V", nc.vector.tensor_tensor(out=ot[:], in0=ot[:], in1=oab[:, 1024:2048], op=ALU.add))
            P.inc("V", nc.vector.tensor_scalar(out=oq[:], in0=ot[:], scalar1=MAGIC, scalar2=MAGIC,
                                               op0=ALU.add, op1=ALU.subtract))
            P.w(nc.scalar, "V")
            P.dma(nc.scalar, "LS", dst[kt * 128:(kt + 1) * 128, :], oq[:])
        P.w(nc.scalar, "LS")
    P.bar()


def _l0_inproj(P, XN, VSM):
    nc, e, B = P.nc, P.e, P.B
    with (
        P.sb("ip_w", [128, 16 * 128], BF16) as ipw,
        P.sb("ip_u", [128, L], F32) as ipu,
        P.sb("ip_uc", [128, 3 * L], F32) as ipuc,
        P.sb("ip_v", [128, L], BF16) as ipv,
        P.sb("ip_g", [128, L], BF16) as ipg,
        P.ps("ip_ps", [128, 512], F32) as ps,
        P.ps("ip_tp", [128, 128], BF16) as ptp,
    ):
        for c in range(8):
            for blk in range(3):
                P.w(nc.gpsimd, "T")
                P.dma(nc.gpsimd, "LG", ipw[:].rearrange("p (t c) -> p t c", c=128),
                      e.ipWT[:, blk * 1024 + c * 128: blk * 1024 + (c + 1) * 128]
                      .rearrange("(t p) c -> p t c", p=128))
                P.w(nc.tensor, "LG")
                j = blk * 8 + c
                for n in range(4):
                    P.w(nc.tensor, "S")
                    for kt in range(16):
                        P.inc("T", nc.tensor.matmul(
                            ps[:], ipw[:, kt * 128:(kt + 1) * 128],
                            XN[:, kt * L + n * 512: kt * L + (n + 1) * 512],
                            start=(kt == 0), stop=(kt == 15)))
                    P.w(nc.scalar, "T")
                    P.w(nc.scalar, "V")
                    P.inc("S", nc.scalar.activation(
                        out=ipu[:, n * 512:(n + 1) * 512], in_=ps[:],
                        func=AF.Identity, bias=B.ipB[:, j:j + 1], scale=1.0))
                P.w(nc.vector, "S")
                u = ipu[:, :]
                uc = ipuc[:, blk * L:(blk + 1) * L]
                P.inc("V", nc.vector.tensor_scalar(out=uc, in0=u, scalar1=B.swT[:, 3 * j + 2:3 * j + 3],
                                                   scalar2=None, op0=ALU.mult))
                P.inc("V", nc.vector.tensor_scalar(out=uc, in0=uc, scalar1=B.sbT[:, j:j + 1],
                                                   scalar2=None, op0=ALU.add))
                P.inc("V", nc.vector.scalar_tensor_tensor(
                    out=uc[:, 2:L], in0=u[:, 0:L - 2], scalar=B.swT[:, 3 * j:3 * j + 1],
                    in1=uc[:, 2:L], op0=ALU.mult, op1=ALU.add))
                P.inc("V", nc.vector.scalar_tensor_tensor(
                    out=uc[:, 1:L], in0=u[:, 0:L - 1], scalar=B.swT[:, 3 * j + 1:3 * j + 2],
                    in1=uc[:, 1:L], op0=ALU.mult, op1=ALU.add))
            P.inc("V", nc.vector.tensor_tensor(out=ipv[:], in0=ipuc[:, 2 * L:3 * L],
                                               in1=ipuc[:, L:2 * L], op=ALU.mult))
            P.inc("V", nc.vector.tensor_scalar(out=ipg[:], in0=ipv[:], scalar1=B.fbT[:, c:c + 1],
                                               scalar2=None, op0=ALU.mult))
            P.w(nc.scalar, "V")
            P.dma(nc.scalar, "LS", e.sc_vb[c * 128:(c + 1) * 128, :], ipg[:])
            P.w(nc.vector, "LS")
            P.inc("V", nc.vector.tensor_copy(out=ipg[:], in_=ipuc[:, 0:L]))
            P.w(nc.scalar, "V")
            P.dma(nc.scalar, "LS", e.sc_x0[c * 128:(c + 1) * 128, :], ipg[:])
            P.w(nc.vector, "LS")
            for stt in range(16):
                P.w(nc.tensor, "V")
                P.inc("T", nc.tensor.transpose(ptp[:], ipv[:, stt * 128:(stt + 1) * 128], B.ident[:]))
                P.w(nc.vector, "T")
                P.inc("V", nc.vector.tensor_copy(
                    out=VSM[:, stt * 1024 + c * 128: stt * 1024 + (c + 1) * 128], in_=ptp[:]))
        P.w(nc.scalar, "LS")
    P.dump("vsm", VSM[:], (128, 16 * 1024), BF16)
    P.bar()


def _sin_rr(P, dst, src, wtmp):
    nc = P.nc
    c1 = float(np.float32(6.28125))
    c2 = float(np.float32(2 * math.pi - 6.28125))
    P.inc("V", nc.vector.tensor_scalar(out=wtmp, in0=src, scalar1=1.0 / (2 * math.pi),
                                       scalar2=MAGIC, op0=ALU.mult, op1=ALU.add))
    P.inc("V", nc.vector.tensor_scalar(out=wtmp, in0=wtmp, scalar1=MAGIC, scalar2=None,
                                       op0=ALU.subtract))
    P.inc("V", nc.vector.scalar_tensor_tensor(out=src, in0=wtmp, scalar=-c1, in1=src,
                                              op0=ALU.mult, op1=ALU.add))
    P.inc("V", nc.vector.scalar_tensor_tensor(out=src, in0=wtmp, scalar=-c2, in1=src,
                                              op0=ALU.mult, op1=ALU.add))
    P.w(nc.scalar, "V")
    P.inc("S", nc.scalar.activation(out=dst, in_=src, func=AF.Sin))
    P.w(nc.vector, "S")


def _l0_filter(P, KSM):
    nc, e, B = P.nc, P.e, P.B
    with (
        P.sb("fl_h", [64, L], F32) as fh,
        P.sb("fl_h2", [64, L], F32) as fh2,
        P.sb("fl_w", [64, L], F32) as fw,
        P.sb("fl_hb", [64, L], BF16) as fhb,
        P.sb("fl_z", [3, L], F32) as fz,
        P.sb("fl_w0", [3, 64], F32) as fw0,
        P.sb("fl_wx", [64, 64], BF16) as fwx,
        P.sb("fl_wxf", [64, 64], F32) as fwxf,
        P.sb("fl_w3", [64, 1024], BF16) as fw3,
        P.sb("fl_w3f", [64, 1024], F32) as fw3f,
        P.sb("fl_dec", [128, 1024], BF16) as fdec,
        P.sb("fl_tv", [1, L], F32) as tvec,
        P.sb("fl_da", [1, 1024], F32) as dabs,
        P.ps("fl_ps", [64, 512], F32) as ps,
        P.ps("fl_pk", [128, 512], F32) as pk,
        P.ps("fl_pd", [128, 512], F32) as pd,
    ):
        P.dma(nc.sync, "LD", fz[:], e.zT3)
        P.dma(nc.sync, "LD", fw0[:], e.w0T)
        P.dma(nc.sync, "LD", tvec[:], e.tvec)
        P.dma(nc.sync, "LD", dabs[:], e.dabs)
        P.w(nc.tensor, "LD")
        for n in range(4):
            P.inc("T", nc.tensor.matmul(ps[:], fw0[:], fz[:, n * 512:(n + 1) * 512],
                                        start=True, stop=True))
            P.w(nc.scalar, "T")
            P.inc("S", nc.scalar.activation(out=fh[:, n * 512:(n + 1) * 512], in_=ps[:],
                                            func=AF.Identity, bias=B.fsml[:, 0:1], scale=1.0))
            P.w(nc.tensor, "S")
        P.w(nc.vector, "S")
        P.inc("V", nc.vector.tensor_scalar(out=fh[:], in0=fh[:], scalar1=B.fsml[:, 3:4],
                                           scalar2=None, op0=ALU.mult))
        _sin_rr(P, fh2[:], fh[:], fw[:])
        P.dump("fh0", fh2[:], (64, L), F32)
        for li, wap in enumerate([e.w1T, e.w2T]):
            P.w(nc.sync, "V")
            P.dma(nc.sync, "LD", fwxf[:], wap)
            P.inc("V", nc.vector.tensor_copy(out=fhb[:], in_=fh2[:]))
            P.w(nc.vector, "LD")
            P.inc("V", nc.vector.tensor_copy(out=fwx[:], in_=fwxf[:]))
            if li == 0:
                P.dump("fhb", fhb[:], (64, L), BF16)
                P.dump("fwx", fwx[:], (64, 64), BF16)
            P.w(nc.tensor, "V")
            for n in range(4):
                P.inc("T", nc.tensor.matmul(ps[:], fwx[:], fhb[:, n * 512:(n + 1) * 512],
                                            start=True, stop=True))
                P.w(nc.scalar, "T")
                P.inc("S", nc.scalar.activation(out=fh[:, n * 512:(n + 1) * 512], in_=ps[:],
                                                func=AF.Identity, bias=B.fsml[:, li + 1:li + 2], scale=1.0))
                P.w(nc.tensor, "S")
            if li == 0:
                P.dump("fe1", fh[:], (64, L), F32)
            P.w(nc.vector, "S")
            P.inc("V", nc.vector.tensor_scalar(out=fh[:], in0=fh[:], scalar1=B.fsml[:, 3:4],
                                               scalar2=None, op0=ALU.mult))
            _sin_rr(P, fh2[:], fh[:], fw[:])
            if li == 0:
                P.dump("fh1", fh2[:], (64, L), F32)
        P.inc("V", nc.vector.tensor_copy(out=fhb[:], in_=fh2[:]))
        P.dump("fh2", fh2[:], (64, L), F32)
        P.dma(nc.sync, "LD", fw3f[:], e.w3T)
        P.w(nc.vector, "LD")
        P.inc("V", nc.vector.tensor_copy(out=fw3[:], in_=fw3f[:]))
        for tt in range(16):
            P.w(nc.tensor, "V")
            for n in range(2):
                P.inc("T", nc.tensor.matmul(pd[:], tvec[:, tt * 128:(tt + 1) * 128],
                                            dabs[:, n * 512:(n + 1) * 512], start=True, stop=True))
                P.w(nc.scalar, "T")
                P.w(nc.scalar, "V")
                P.inc("S", nc.scalar.activation(out=fdec[:, n * 512:(n + 1) * 512], in_=pd[:],
                                                func=AF.Exp, scale=-1.0))
                P.w(nc.tensor, "S")
            P.w(nc.tensor, "S")
            for n in range(2):
                P.w(nc.tensor, "V")
                P.inc("T", nc.tensor.matmul(pk[:], fhb[:, tt * 128:(tt + 1) * 128],
                                            fw3[:, n * 512:(n + 1) * 512], start=True, stop=True))
                P.w(nc.vector, "T")
                P.inc("V", nc.vector.tensor_tensor(
                    out=KSM[:, tt * 1024 + n * 512: tt * 1024 + (n + 1) * 512],
                    in0=pk[:], in1=fdec[:, n * 512:(n + 1) * 512], op=ALU.mult))
    P.dump("ksm", KSM[:], (128, 16 * 1024), BF16)
    P.bar()


def _l0_dft(P, VSM, KSM, YSM):
    nc, e, B = P.nc, P.e, P.B
    with (
        P.sb("df_Uc", [128, NFT * 1024], BF16) as Uc,
        P.sb("df_Us", [128, NFT * 1024], BF16) as Us,
        P.sb("df_fs", [128, 2 * 2176], BF16) as fstage,
        P.sb("df_k", [128, 2 * 1024], BF16) as kcs,
        P.sb("df_t", [128, 4 * 1024], BF16) as tmp4,
        P.ps("df_p0", [128, 512], F32) as pA0,
        P.ps("df_p1", [128, 512], F32) as pA1,
        P.ps("df_p2", [128, 512], F32) as pA2,
        P.ps("df_p3", [128, 512], F32) as pA3,
        P.ps("df_p4", [128, 512], F32) as pB0,
        P.ps("df_p5", [128, 512], F32) as pB1,
        P.ps("df_p6", [128, 512], F32) as pB2,
        P.ps("df_p7", [128, 512], F32) as pB3,
    ):
        for ft in range(NFT):
            for ci, Fmat in enumerate([e.Fc, e.Fs]):
                P.w(nc.gpsimd, "T")
                P.dma(nc.gpsimd, "LG", fstage[:, ci * 2176:ci * 2176 + 2048].rearrange("p (t c) -> p t c", c=128),
                      Fmat[:, ft * 128:(ft + 1) * 128].rearrange("(t p) c -> p t c", p=128))
            P.w(nc.tensor, "LG")
            P.w(nc.tensor, "V")
            pu = [[pA0, pA1], [pA2, pA3]]
            pk = [[pB0, pB1], [pB2, pB3]]
            for ci in range(2):
                for kt in range(16):
                    wt = fstage[:, ci * 2176 + kt * 128: ci * 2176 + (kt + 1) * 128]
                    for n in range(2):
                        P.inc("T", nc.tensor.matmul(
                            pu[ci][n][:], wt, VSM[:, kt * 1024 + n * 512: kt * 1024 + (n + 1) * 512],
                            start=(kt == 0), stop=(kt == 15)))
                        P.inc("T", nc.tensor.matmul(
                            pk[ci][n][:], wt, KSM[:, kt * 1024 + n * 512: kt * 1024 + (n + 1) * 512],
                            start=(kt == 0), stop=(kt == 15)))
            P.w(nc.vector, "T")
            for ci, U in enumerate([Uc, Us]):
                for n in range(2):
                    sl = slice(ft * 1024 + n * 512, ft * 1024 + (n + 1) * 512)
                    P.inc("V", nc.vector.tensor_copy(out=U[:, sl], in_=pu[ci][n][:]))
                    P.inc("V", nc.vector.tensor_copy(
                        out=kcs[:, ci * 1024 + n * 512: ci * 1024 + (n + 1) * 512], in_=pk[ci][n][:]))
            uf = slice(ft * 1024, (ft + 1) * 1024)
            P.inc("V", nc.vector.tensor_tensor(out=tmp4[:, 0:1024], in0=Uc[:, uf], in1=kcs[:, 0:1024], op=ALU.mult))
            P.inc("V", nc.vector.tensor_tensor(out=tmp4[:, 1024:2048], in0=Us[:, uf], in1=kcs[:, 1024:2048], op=ALU.mult))
            P.inc("V", nc.vector.tensor_tensor(out=tmp4[:, 2048:3072], in0=Uc[:, uf], in1=kcs[:, 1024:2048], op=ALU.mult))
            P.inc("V", nc.vector.tensor_tensor(out=tmp4[:, 3072:4096], in0=Us[:, uf], in1=kcs[:, 0:1024], op=ALU.mult))
            P.inc("V", nc.vector.tensor_tensor(out=Uc[:, uf], in0=tmp4[:, 0:1024], in1=tmp4[:, 1024:2048], op=ALU.subtract))
            P.inc("V", nc.vector.tensor_tensor(out=Us[:, uf], in0=tmp4[:, 2048:3072], in1=tmp4[:, 3072:4096], op=ALU.add))
        P.bar()
        for tt in range(16):
            for ci, Gmat in enumerate([e.Gc, e.Gs]):
                P.w(nc.gpsimd, "T")
                P.dma(nc.gpsimd, "LG", fstage[:, ci * 2176: ci * 2176 + NFT * 128].rearrange("p (t c) -> p t c", c=128),
                      Gmat[:, tt * 128:(tt + 1) * 128].rearrange("(t p) c -> p t c", p=128))
            P.w(nc.tensor, "LG")
            P.w(nc.tensor, "V")
            for n in range(2):
                pbank = [pA0, pA1][n]
                nmm = 0
                for ci, PP in enumerate([Uc, Us]):
                    for ft in range(NFT):
                        P.inc("T", nc.tensor.matmul(
                            pbank[:], fstage[:, ci * 2176 + ft * 128: ci * 2176 + (ft + 1) * 128],
                            PP[:, ft * 1024 + n * 512: ft * 1024 + (n + 1) * 512],
                            start=(nmm == 0), stop=(nmm == 2 * NFT - 1)))
                        nmm += 1
            P.w(nc.vector, "T")
            for n in range(2):
                P.inc("V", nc.vector.tensor_copy(
                    out=YSM[:, tt * 1024 + n * 512: tt * 1024 + (n + 1) * 512], in_=[pA0, pA1][n][:]))
    P.dump("ysm", YSM[:], (128, 16 * 1024), BF16)
    P.bar()


def _gather16(P, shmem, dst):
    nc, e, B = P.nc, P.e, P.B
    shv = shmem.rearrange("r (c q) -> (r c) q", q=1024)
    with P.sb("g16", [128, 1024], BF16) as GST:
        for j in range(16):
            P.w(nc.gpsimd, "V")
            P.inc("LG", nc.gpsimd.indirect_dma_start(
                out=GST[:], out_offset=None,
                in_=shv, in_offset=IndirectOffsetOnAxis(getattr(B, "idxg%d" % j)[:, 0:1], 0)), 16)
            P.w(nc.vector, "LG")
            P.inc("V", nc.vector.tensor_copy(out=dst[:, j * 1024:(j + 1) * 1024], in_=GST[:]))
    P.w(nc.gpsimd, "V")


def _l0_gate_exchange(P, YSM, YG, YA):
    nc, e, B = P.nc, P.e, P.B
    with (
        P.sb("gt_x0", [128, L], BF16) as gx0,
        P.sb("gt_vb", [128, L], BF16) as gvb,
        P.sb("gt_yf", [128, L], BF16) as gyf,
        P.ps("gt_tp", [128, 128], BF16) as ptp,
    ):
        for c in range(8):
            P.w(nc.sync, "V")
            P.dma(nc.sync, "LD", gx0[:], e.sc_x0[c * 128:(c + 1) * 128, :])
            P.dma(nc.sync, "LD", gvb[:], e.sc_vb[c * 128:(c + 1) * 128, :])
            for stt in range(16):
                P.w(nc.tensor, "V")
                P.inc("T", nc.tensor.transpose(
                    ptp[:], YSM[:, stt * 1024 + c * 128: stt * 1024 + (c + 1) * 128], B.ident[:]))
                P.w(nc.vector, "T")
                P.inc("V", nc.vector.tensor_copy(out=gyf[:, stt * 128:(stt + 1) * 128], in_=ptp[:]))
            P.w(nc.vector, "LD")
            P.inc("V", nc.vector.tensor_tensor(out=gyf[:], in0=gyf[:], in1=gvb[:], op=ALU.add))
            P.inc("V", nc.vector.tensor_tensor(out=YG[:, c * L:(c + 1) * L], in0=gyf[:], in1=gx0[:], op=ALU.mult))
    P.dump("yg", YG[:], (128, 16 * 1024), BF16)
    P.w(nc.gpsimd, "V")
    P.inc("LG", nc.gpsimd.indirect_dma_start(
        out=e.sh_y, out_offset=IndirectOffsetOnAxis(B.idxs[:, 0:1], 0),
        in_=YG[:], in_offset=None), 16)
    P.w(nc.gpsimd, "LG")
    P.core_bar()
    _gather16(P, e.sh_y, YA)
    P.dump("ya", YA[:], (128, 16 * 1024), BF16)
    P.bar()


def _l0_outproj(P, YA):
    nc, e, B = P.nc, P.e, P.B
    with (
        P.sb("op_w", [128, 16 * 128], BF16) as opw,
        P.sb("op_s", [128, 1024], F32) as ops,
        P.ps("op_p0", [128, 512], F32) as p0,
        P.ps("op_p1", [128, 512], F32) as p1,
    ):
        for ot in range(16):
            P.w(nc.gpsimd, "T")
            P.dma(nc.gpsimd, "LG", opw[:].rearrange("p (t c) -> p t c", c=128),
                  e.opWT[:, ot * 128:(ot + 1) * 128].rearrange("(t p) c -> p t c", p=128))
            P.w(nc.tensor, "LG")
            P.w(nc.tensor, "S")
            for n in range(2):
                for kt in range(16):
                    P.inc("T", nc.tensor.matmul(
                        [p0, p1][n][:], opw[:, kt * 128:(kt + 1) * 128],
                        YA[:, kt * 1024 + n * 512: kt * 1024 + (n + 1) * 512],
                        start=(kt == 0), stop=(kt == 15)))
            P.w(nc.scalar, "T")
            P.w(nc.scalar, "LS")
            for n in range(2):
                P.inc("S", nc.scalar.activation(out=ops[:, n * 512:(n + 1) * 512], in_=[p0, p1][n][:],
                                                func=AF.Identity, bias=B.opB[:, ot:ot + 1], scale=1.0))
            P.dma(nc.scalar, "LS", e.sc_s0[ot * 128:(ot + 1) * 128, :], ops[:])
        P.w(nc.scalar, "LS")
    P.bar()


def _ffn(P, layer):
    nc, e, B = P.nc, P.e, P.B
    gw = [e.gwT0, e.gwT1][layer]
    uw = [e.uwT0, e.uwT1][layer]
    dw = [e.dwT0, e.dwT1][layer]
    xqs = [e.sc_xq0, e.sc_xq1][layer]
    ssc = [e.sc_s0, e.sc_s1][layer]
    xsrc = [e.xoT, e.sc_x1][layer]
    inner = [AF.Silu, AF.Sigmoid][layer]
    iw_g, iw_u, iw_d = 3 * layer + 0, 3 * layer + 1, 3 * layer + 2
    rsl = slice(layer * 1024, (layer + 1) * 1024)

    with (
        P.sb("ff_mx", [128, 1024], F32) as RMX,
        P.sb("ff_mn", [128, 1024], F32) as RMN,
        P.sb("ff_gb", [128, 2 * 1024], F32) as GB,
        P.sb("ff_row", [1, 3 * 1024], F32) as R,
    ):
        def row(i):
            return R[:, i * 1024:(i + 1) * 1024]
        P.inc("V", nc.vector.tensor_scalar(out=row(0), in0=B.rsx[0:1, rsl],
                                           scalar1=B.swv[0:1, iw_g:iw_g + 1], scalar2=None, op0=ALU.mult))
        P.inc("V", nc.vector.tensor_scalar(out=row(1), in0=B.rsx[0:1, rsl],
                                           scalar1=B.swv[0:1, iw_u:iw_u + 1], scalar2=None, op0=ALU.mult))
        _bcast(P, row(0), GB[:, 0:1024], 1024)
        _bcast(P, row(1), GB[:, 1024:2048], 1024)
        P.inc("V", nc.vector.memset(RMX[:], 0.0))
        P.inc("V", nc.vector.memset(RMN[:], 0.0))

        with ExitStack() as es:
            XQ = es.enter_context(P.sb("ff_xq", [128, 16 * 1024], BF16))
            WQ = es.enter_context(P.sb("ff_wq", [128, 16 * 1024], BF16))
            WLD = es.enter_context(P.sb("ff_ld", [128, 1024], F32))
            QT = es.enter_context(P.sb("ff_qt", [128, 1024], F32))
            GTMP = es.enter_context(P.sb("ff_gt", [128, 512], F32))
            HC = es.enter_context(P.sb("ff_hc", [128, 8 * 1024], F32))
            banks = [es.enter_context(P.ps("ff_b%d" % i, [128, 512], F32))
                     for i in range(8)]
            for kt in range(16):
                P.dma(nc.sync, "LD", XQ[:, kt * 1024:(kt + 1) * 1024], xqs[kt * 128:(kt + 1) * 128, :])

            def quant_tiles(wap, iw, chunk):
                P.w(nc.vector, "T")
                P.w(nc.scalar, "T")
                for kt in range(16):
                    P.w(nc.sync, "S")
                    P.dma(nc.sync, "LD", WLD[:],
                          wap[kt * 128:(kt + 1) * 128, chunk * 1024:(chunk + 1) * 1024])
                    P.w(nc.scalar, "LD")
                    P.w(nc.scalar, "V")
                    P.inc("S", nc.scalar.activation(out=QT[:], in_=WLD[:],
                                                    func=AF.Identity, bias=B.magic[:, 0:1],
                                                    scale=B.pb[:, iw:iw + 1]))
                    P.w(nc.vector, "S")
                    P.inc("V", nc.vector.tensor_scalar(out=QT[:], in0=QT[:], scalar1=float(MAGIC - 1.0),
                                                       scalar2=float(MAGIC + 1.0), op0=ALU.max, op1=ALU.min))
                    P.inc("V", nc.vector.tensor_scalar(out=WQ[:, kt * 1024:(kt + 1) * 1024], in0=QT[:],
                                                       scalar1=MAGIC, scalar2=None, op0=ALU.subtract))

            P.w(nc.tensor, "LD")
            for chunk in range(8):
                quant_tiles(gw, iw_g, chunk)
                for rh in range(2):
                    P.w(nc.tensor, "V")
                    P.w(nc.tensor, "S")
                    for kt in range(16):
                        for oc in range(8):
                            P.inc("T", nc.tensor.matmul(
                                banks[oc][:], WQ[:, kt * 1024 + oc * 128: kt * 1024 + (oc + 1) * 128],
                                XQ[:, kt * 1024 + rh * 512: kt * 1024 + (rh + 1) * 512],
                                start=(kt == 0), stop=(kt == 15)))
                    P.w(nc.vector, "T")
                    for oc in range(8):
                        sl = slice(oc * 1024 + rh * 512, oc * 1024 + (rh + 1) * 512)
                        P.w(nc.vector, "S")
                        P.inc("V", nc.vector.tensor_tensor(
                            out=GTMP[:], in0=banks[oc][:], in1=GB[:, rh * 512:(rh + 1) * 512], op=ALU.mult))
                        P.w(nc.scalar, "V")
                        P.w(nc.scalar, "LS")
                        P.inc("S", nc.scalar.activation(out=HC[:, sl], in_=GTMP[:], func=inner))
                if chunk == 0 and layer == 0:
                    P.dump("hcg0", HC[:], (128, 8 * 1024), F32)
                quant_tiles(uw, iw_u, chunk)
                for rh in range(2):
                    P.w(nc.tensor, "V")
                    P.w(nc.tensor, "S")
                    for kt in range(16):
                        for oc in range(8):
                            P.inc("T", nc.tensor.matmul(
                                banks[oc][:], WQ[:, kt * 1024 + oc * 128: kt * 1024 + (oc + 1) * 128],
                                XQ[:, kt * 1024 + rh * 512: kt * 1024 + (rh + 1) * 512],
                                start=(kt == 0), stop=(kt == 15)))
                    P.w(nc.vector, "T")
                    P.w(nc.vector, "S")
                    for oc in range(8):
                        sl = slice(oc * 1024 + rh * 512, oc * 1024 + (rh + 1) * 512)
                        P.inc("V", nc.vector.tensor_tensor(out=HC[:, sl], in0=HC[:, sl],
                                                           in1=banks[oc][:], op=ALU.mult))
                        P.inc("V", nc.vector.tensor_tensor(out=HC[:, sl], in0=HC[:, sl],
                                                           in1=GB[:, 1024 + rh * 512: 1024 + (rh + 1) * 512],
                                                           op=ALU.mult))
                        P.inc("V", nc.vector.tensor_tensor(out=RMX[:, rh * 512:(rh + 1) * 512],
                                                           in0=RMX[:, rh * 512:(rh + 1) * 512],
                                                           in1=HC[:, sl], op=ALU.max))
                        P.inc("V", nc.vector.tensor_tensor(out=RMN[:, rh * 512:(rh + 1) * 512],
                                                           in0=RMN[:, rh * 512:(rh + 1) * 512],
                                                           in1=HC[:, sl], op=ALU.min))
                if chunk == 0 and layer == 0:
                    P.dump("hc0", HC[:], (128, 8 * 1024), F32)
                P.w(nc.scalar, "V")
                P.dma(nc.scalar, "LS",
                      e.sc_h[chunk * 1024:(chunk + 1) * 1024, :].rearrange("(o p) c -> p o c", p=128),
                      HC[:].rearrange("p (o c) -> p o c", c=1024))
            P.w(nc.scalar, "LS")
        P.bar()

        with (
            P.sb("fh_cb", [128, 2], F32) as colbuf,
            P.ps("fh_pT", [128, 128], F32) as psT,
            P.ps("fh_pR", [1, 128], F32) as psR,
        ):
            _pmax_rows(P, RMX, 1024, row(0), row(2), colbuf, psT, psR)
            _pmax_rows(P, RMN, 1024, row(2), row(1), colbuf, psT, psR)
        P.inc("V", nc.vector.tensor_scalar(out=row(1), in0=row(1), scalar1=-1.0, scalar2=None, op0=ALU.mult))
        P.inc("V", nc.vector.tensor_tensor(out=row(0), in0=row(0), in1=row(1), op=ALU.max))
        P.inc("V", nc.vector.tensor_scalar(out=row(0), in0=row(0), scalar1=1e-5, scalar2=None, op0=ALU.max))
        P.inc("V", nc.vector.reciprocal(out=row(1), in_=row(0)))
        P.vbounce()
        P.inc("V", nc.vector.tensor_scalar(out=row(1), in0=row(1), scalar1=127.0, scalar2=None, op0=ALU.mult))
        P.inc("V", nc.vector.tensor_scalar(out=row(0), in0=row(0), scalar1=1.0 / 127.0,
                                           scalar2=None, op0=ALU.mult))
        P.inc("V", nc.vector.tensor_scalar(out=row(0), in0=row(0),
                                           scalar1=B.swv[0:1, iw_d:iw_d + 1], scalar2=None, op0=ALU.mult))
        _bcast(P, row(0), GB[:, 0:1024], 1024)
        _bcast(P, row(1), GB[:, 1024:2048], 1024)
        if layer == 0:
            P.dump_dram("hh0", e.sc_h, (8192, 1024), F32)
            P.dump("gbd", GB[:], (128, 2048), F32)

        with (
            P.sb("fd_ld", [128, 2048], F32) as DLD,
            P.sb("fd_qt", [128, 2048], F32) as DQT,
            P.sb("fd_wq", [128, 2048], BF16) as DWQ,
        ):
            for kt in range(64):
                if kt >= 1:
                    P.w(nc.sync, "S")
                P.dma(nc.sync, "LD", DLD[:], dw[kt * 128:(kt + 1) * 128, :])
                P.w(nc.scalar, "LD")
                P.w(nc.scalar, "V")
                P.inc("S", nc.scalar.activation(out=DQT[:], in_=DLD[:],
                                                func=AF.Identity, bias=B.magic[:, 0:1],
                                                scale=B.pb[:, iw_d:iw_d + 1]))
                P.w(nc.vector, "S")
                P.w(nc.vector, "LS")
                P.inc("V", nc.vector.tensor_scalar(out=DQT[:], in0=DQT[:],
                                                   scalar1=float(MAGIC - 1.0), scalar2=float(MAGIC + 1.0),
                                                   op0=ALU.max, op1=ALU.min))
                P.inc("V", nc.vector.tensor_scalar(out=DWQ[:], in0=DQT[:],
                                                   scalar1=MAGIC, scalar2=None, op0=ALU.subtract))
                P.w(nc.scalar, "V")
                P.dma(nc.scalar, "LS", e.sc_dw[kt * 128:(kt + 1) * 128, :], DWQ[:])
            P.w(nc.scalar, "LS")
        P.bar()

        with ExitStack() as es:
            HINT = es.enter_context(P.sb("ff_hi", [128, 64 * 512], BF16))
            DST = es.enter_context(P.sb("ff_ds", [128, 64 * 256], BF16))
            Q2 = es.enter_context(P.sb("ff_q2", [128, 2 * 512], F32))
            G1 = es.enter_context(P.sb("ff_g", [128, 512], F32))
            XR = es.enter_context(P.sb("ff_xr", [128, 512], F32))
            SR = es.enter_context(P.sb("ff_sr", [128, 512], F32))
            OT = es.enter_context(P.sb("ff_o", [128, 512], F32))
            d0 = es.enter_context(P.ps("ff_d0", [128, 512], F32))
            d1 = es.enter_context(P.ps("ff_d1", [128, 512], F32))
            for rh in range(2):
                for kt in range(64):
                    P.w(nc.sync, "V")
                    P.dma(nc.sync, "LD", Q2[:, 0:512], e.sc_h[kt * 128:(kt + 1) * 128,
                                                             rh * 512:(rh + 1) * 512])
                    P.w(nc.vector, "LD")
                    P.inc("V", nc.vector.tensor_tensor(out=Q2[:, 512:1024], in0=Q2[:, 0:512],
                                                       in1=GB[:, 1024 + rh * 512: 1024 + (rh + 1) * 512],
                                                       op=ALU.mult))
                    P.inc("V", nc.vector.tensor_scalar(out=HINT[:, kt * 512:(kt + 1) * 512],
                                                       in0=Q2[:, 512:1024], scalar1=MAGIC, scalar2=MAGIC,
                                                       op0=ALU.add, op1=ALU.subtract))
                for op8 in range(8):
                    P.w(nc.gpsimd, "T")
                    P.dma(nc.gpsimd, "LG", DST[:].rearrange("p (t c) -> p t c", c=256),
                          e.sc_dw[:, op8 * 256:(op8 + 1) * 256].rearrange("(t p) c -> p t c", p=128))
                    P.w(nc.tensor, "LG")
                    P.w(nc.tensor, "V")
                    P.w(nc.tensor, "S")
                    for kt in range(64):
                        P.inc("T", nc.tensor.matmul(d0[:], DST[:, kt * 256:kt * 256 + 128],
                                                    HINT[:, kt * 512:(kt + 1) * 512],
                                                    start=(kt == 0), stop=(kt == 63)))
                        P.inc("T", nc.tensor.matmul(d1[:], DST[:, kt * 256 + 128:kt * 256 + 256],
                                                    HINT[:, kt * 512:(kt + 1) * 512],
                                                    start=(kt == 0), stop=(kt == 63)))
                    for bi, bank in enumerate((d0, d1)):
                        ot = op8 * 2 + bi
                        P.w(nc.sync, "V")
                        P.dma(nc.sync, "LD", XR[:], xsrc[ot * 128:(ot + 1) * 128, rh * 512:(rh + 1) * 512])
                        P.dma(nc.sync, "LD", SR[:], ssc[ot * 128:(ot + 1) * 128, rh * 512:(rh + 1) * 512])
                        P.w(nc.vector, "T")
                        P.w(nc.vector, "S")
                        P.w(nc.vector, "LS")
                        P.w(nc.vector, "ST")
                        P.inc("V", nc.vector.tensor_tensor(out=G1[:], in0=bank[:],
                                                           in1=GB[:, rh * 512:(rh + 1) * 512], op=ALU.mult))
                        P.w(nc.scalar, "V")
                        P.w(nc.scalar, "LS")
                        P.w(nc.scalar, "ST")
                        P.inc("S", nc.scalar.activation(out=OT[:], in_=G1[:], func=AF.Silu))
                        P.w(nc.vector, "LD")
                        P.w(nc.vector, "S")
                        P.inc("V", nc.vector.tensor_tensor(out=G1[:], in0=OT[:], in1=SR[:], op=ALU.mult))
                        P.inc("V", nc.vector.tensor_tensor(out=OT[:], in0=G1[:], in1=XR[:], op=ALU.add))
                        P.w(nc.scalar, "V")
                        if layer == 0:
                            P.dma(nc.scalar, "LS", e.sc_x1[ot * 128:(ot + 1) * 128, rh * 512:(rh + 1) * 512], OT[:])
                        else:
                            P.dma(nc.scalar, "ST", e.outT[ot * 128:(ot + 1) * 128, rh * 512:(rh + 1) * 512], OT[:])
            P.w(nc.scalar, "LS")
            P.w(nc.scalar, "ST")
    P.bar()


def _e1_scatter(P):
    nc, e, B = P.nc, P.e, P.B
    with P.sb("e1_x", [128, 16 * 1024], F32) as X1:
        for kt in range(16):
            P.dma(nc.sync, "LD", X1[:, kt * 1024:(kt + 1) * 1024], e.sc_x1[kt * 128:(kt + 1) * 128, :])
        P.w(nc.gpsimd, "LD")
        P.inc("LG", nc.gpsimd.indirect_dma_start(
            out=e.sh_x1, out_offset=IndirectOffsetOnAxis(B.idxs[:, 0:1], 0),
            in_=X1[:], in_offset=None), 16)
        P.w(nc.gpsimd, "LG")
    P.core_bar()


def _l1_qkv(P, XN1, Q, KT, VSM2):
    nc, e, B = P.nc, P.e, P.B
    with (
        P.sb("qk_w", [128, 16 * 128], BF16) as qw,
        P.sb("qk_t", [128, 512], F32) as qt,
        P.sb("qk_v", [128, 2 * L], BF16) as VF,
        P.ps("qk_p", [128, 512], F32) as ps,
        P.ps("qk_tp", [128, 128], BF16) as ptp,
    ):
        jobs = []
        for ot in range(8):
            jobs.append((e.wqm, ot, B.bqm[:, ot:ot + 1], True, Q, ot))
        for ot in range(2):
            jobs.append((e.wkm, ot, B.bkm[:, ot:ot + 1], True, KT, ot))
        for ot in range(2):
            jobs.append((e.wvm, ot, B.bvm[:, ot:ot + 1], False, VF, ot))
        for wap, ot, bias, addpe, dstb, dsto in jobs:
            P.w(nc.gpsimd, "T")
            P.dma(nc.gpsimd, "LG", qw[:].rearrange("p (t c) -> p t c", c=128),
                  wap[:, ot * 128:(ot + 1) * 128].rearrange("(t p) c -> p t c", p=128))
            P.w(nc.tensor, "LG")
            for n in range(4):
                P.w(nc.tensor, "V")
                for kt in range(16):
                    P.inc("T", nc.tensor.matmul(
                        ps[:], qw[:, kt * 128:(kt + 1) * 128],
                        XN1[:, kt * L + n * 512: kt * L + (n + 1) * 512],
                        start=(kt == 0), stop=(kt == 15)))
                P.w(nc.scalar, "T")
                P.w(nc.scalar, "V")
                P.inc("S", nc.scalar.activation(out=qt[:], in_=ps[:], func=AF.Identity,
                                                bias=bias, scale=1.0))
                P.w(nc.vector, "S")
                if addpe:
                    P.inc("V", nc.vector.tensor_tensor(
                        out=dstb[:, dsto * L + n * 512: dsto * L + (n + 1) * 512],
                        in0=qt[:], in1=B.peT[:, n * 512:(n + 1) * 512], op=ALU.add))
                else:
                    P.inc("V", nc.vector.tensor_copy(
                        out=dstb[:, dsto * L + n * 512: dsto * L + (n + 1) * 512], in_=qt[:]))
        for g in range(2):
            for stt in range(16):
                P.w(nc.tensor, "V")
                P.inc("T", nc.tensor.transpose(ptp[:], VF[:, g * L + stt * 128: g * L + (stt + 1) * 128],
                                               B.ident[:]))
                P.w(nc.vector, "T")
                P.inc("V", nc.vector.tensor_copy(
                    out=VSM2[:, stt * 256 + g * 128: stt * 256 + (g + 1) * 128], in_=ptp[:]))
        with (
            P.sb("qk_n", [128, 2048], BF16) as qq,
            P.sb("qk_r", [1, 96], F32) as R,
            P.ps("qk_n1", [1, 512], F32) as pn,
        ):
            for g in range(2):
                for n in range(4):
                    P.inc("V", nc.vector.tensor_tensor(out=qq[:, n * 512:(n + 1) * 512],
                                                       in0=KT[:, g * L + n * 512: g * L + (n + 1) * 512],
                                                       in1=KT[:, g * L + n * 512: g * L + (n + 1) * 512],
                                                       op=ALU.mult))
                for n in range(4):
                    P.w(nc.tensor, "V")
                    P.inc("T", nc.tensor.matmul(pn[:], B.onesb[:], qq[:, n * 512:(n + 1) * 512],
                                                start=True, stop=True))
                    P.w(nc.vector, "T")
                    P.inc("V", nc.vector.tensor_reduce(out=R[:, 44 + n:45 + n], in_=pn[:],
                                                       axis=AXX, op=ALU.max))
                P.vbounce()
                P.inc("V", nc.vector.tensor_tensor(out=R[:, 44:46], in0=R[:, 44:46],
                                                   in1=R[:, 46:48], op=ALU.max))
                P.inc("V", nc.vector.tensor_tensor(out=R[:, 40 + g:41 + g], in0=R[:, 44:45],
                                                   in1=R[:, 45:46], op=ALU.max))
            for i in range(8):
                for n in range(4):
                    P.inc("V", nc.vector.tensor_tensor(out=qq[:, n * 512:(n + 1) * 512],
                                                       in0=Q[:, i * L + n * 512: i * L + (n + 1) * 512],
                                                       in1=Q[:, i * L + n * 512: i * L + (n + 1) * 512],
                                                       op=ALU.mult))
                for c in range(4):
                    P.w(nc.tensor, "V")
                    P.inc("T", nc.tensor.matmul(pn[:], B.onesb[:], qq[:, c * 512:(c + 1) * 512],
                                                start=True, stop=True))
                    P.w(nc.vector, "T")
                    P.inc("V", nc.vector.tensor_reduce(out=R[:, i * 4 + c:i * 4 + c + 1], in_=pn[:],
                                                       axis=AXX, op=ALU.max))
            P.w(nc.scalar, "V")
            P.inc("S", nc.scalar.sqrt(out=R[:, 0:32], in_=R[:, 0:32]))
            P.inc("S", nc.scalar.sqrt(out=R[:, 40:42], in_=R[:, 40:42]))
            P.w(nc.vector, "S")
            for i in range(8):
                g = i // 4
                P.inc("V", nc.vector.tensor_scalar(
                    out=R[:, 48 + i * 4:48 + (i + 1) * 4], in0=R[:, i * 4:(i + 1) * 4],
                    scalar1=R[:, 40 + g:41 + g], scalar2=None, op0=ALU.mult))
                P.inc("V", nc.vector.tensor_scalar(
                    out=R[:, 48 + i * 4:48 + (i + 1) * 4], in0=R[:, 48 + i * 4:48 + (i + 1) * 4],
                    scalar1=-ISQ, scalar2=None, op0=ALU.mult))
            with P.ps("qk_nb", [128, 32], F32) as pnb:
                P.w(nc.tensor, "V")
                P.inc("T", nc.tensor.matmul(pnb[:], B.ones1f[:], R[:, 48:80], start=True, stop=True))
                P.w(nc.vector, "T")
                P.inc("V", nc.vector.tensor_copy(out=B.nmb[:], in_=pnb[:]))
    P.bar()


def _l1_attn(P, Q, KT, VSM2, ATTN):
    nc, e, B = P.nc, P.e, P.B
    with (
        P.sb("at_pr", [128, 16 * 512], BF16) as PR,
        P.sb("at_rs", [1, 512], F32) as RS,
        P.sb("at_rb", [1, 512], BF16) as RSB,
        P.sb("at_rbb", [128, 512], F32) as RBB,
        P.ps("at_s0", [128, 512], F32) as s0,
        P.ps("at_s1", [128, 512], F32) as s1,
        P.ps("at_su", [1, 512], F32) as su,
        P.ps("at_pv", [128, 512], F32) as pv,
        P.ps("at_bc", [128, 512], F32) as bc,
    ):
        for i in range(8):
            g = i // 4
            for c in range(4):
                nk = 4 * (c + 1)
                P.w(nc.scalar, "T")
                for kt in range(nk):
                    sb = [s0, s1][kt % 2]
                    if kt >= 2:
                        P.w(nc.tensor, "S")
                    P.inc("T", nc.tensor.matmul(
                        sb[:], KT[:, g * L + kt * 128: g * L + (kt + 1) * 128],
                        Q[:, i * L + c * 512: i * L + (c + 1) * 512], start=True, stop=True))
                    P.w(nc.scalar, "T")
                    P.inc("S", nc.scalar.activation(
                        out=PR[:, kt * 512:(kt + 1) * 512], in_=sb[:], func=AF.Exp,
                        bias=B.nmb[:, i * 4 + c: i * 4 + c + 1], scale=ISQ))
                P.w(nc.vector, "S")
                for j in range(4):
                    kt = nk - 4 + j
                    P.inc("V", nc.vector.tensor_tensor(
                        out=PR[:, kt * 512:(kt + 1) * 512], in0=PR[:, kt * 512:(kt + 1) * 512],
                        in1=B.masks[:, j * 512:(j + 1) * 512], op=ALU.mult))
                P.w(nc.tensor, "V")
                for kt in range(nk):
                    P.inc("T", nc.tensor.matmul(su[:], B.onesb[:], PR[:, kt * 512:(kt + 1) * 512],
                                                start=(kt == 0), stop=(kt == nk - 1)))
                P.w(nc.vector, "T")
                P.inc("V", nc.vector.reciprocal(out=RS[:], in_=su[:]))
                P.vbounce()
                P.inc("V", nc.vector.tensor_copy(out=RSB[:], in_=RS[:]))
                P.w(nc.tensor, "V")
                P.inc("T", nc.tensor.matmul(bc[:], B.ones1b[:], RSB[:], start=True, stop=True))
                for kt in range(nk):
                    P.inc("T", nc.tensor.matmul(pv[:], VSM2[:, kt * 256 + g * 128: kt * 256 + (g + 1) * 128],
                                                PR[:, kt * 512:(kt + 1) * 512],
                                                start=(kt == 0), stop=(kt == nk - 1)))
                P.w(nc.vector, "T")
                P.inc("V", nc.vector.tensor_copy(out=RBB[:], in_=bc[:]))
                P.inc("V", nc.vector.tensor_tensor(out=ATTN[:, i * L + c * 512: i * L + (c + 1) * 512],
                                                   in0=pv[:], in1=RBB[:], op=ALU.mult))
    P.bar()


def _e2_exchange(P, ATTN):
    nc, e, B = P.nc, P.e, P.B
    P.w(nc.gpsimd, "V")
    P.inc("LG", nc.gpsimd.indirect_dma_start(
        out=e.sh_at, out_offset=IndirectOffsetOnAxis(B.idxs[:, 0:1], 0),
        in_=ATTN[:], in_offset=None), 16)
    P.w(nc.gpsimd, "LG")
    P.core_bar()


def _e2_gather_outproj(P, AASM):
    nc, e, B = P.nc, P.e, P.B
    _gather16(P, e.sh_at, AASM)
    P.w(nc.tensor, "LG")
    with (
        P.sb("o1_w", [128, 16 * 128], BF16) as opw,
        P.sb("o1_s", [128, 1024], F32) as ops,
        P.ps("o1_p0", [128, 512], F32) as p0,
        P.ps("o1_p1", [128, 512], F32) as p1,
    ):
        for ot in range(16):
            P.w(nc.gpsimd, "T")
            P.dma(nc.gpsimd, "LG", opw[:].rearrange("p (t c) -> p t c", c=128),
                  e.woP[:, ot * 128:(ot + 1) * 128].rearrange("(t p) c -> p t c", p=128))
            P.w(nc.tensor, "LG")
            P.w(nc.tensor, "S")
            for n in range(2):
                for kt in range(16):
                    P.inc("T", nc.tensor.matmul(
                        [p0, p1][n][:], opw[:, kt * 128:(kt + 1) * 128],
                        AASM[:, kt * 1024 + n * 512: kt * 1024 + (n + 1) * 512],
                        start=(kt == 0), stop=(kt == 15)))
            P.w(nc.scalar, "T")
            P.w(nc.scalar, "LS")
            for n in range(2):
                P.inc("S", nc.scalar.activation(out=ops[:, n * 512:(n + 1) * 512], in_=[p0, p1][n][:],
                                                func=AF.Identity, bias=0.0, scale=1.0))
            P.dma(nc.scalar, "LS", e.sc_s1[ot * 128:(ot + 1) * 128, :], ops[:])
        P.w(nc.scalar, "LS")
    P.bar()


def _finish(P):
    nc = P.nc
    P.w(nc.scalar, "ST")
    P.bar()


def kernel(_debug=(), _raw=False, **inputs):
    consts = _host_constants()
    in_maps = [_per_core_inputs(c, inputs, consts) for c in range(8)]
    nc = build(debug=_debug)
    res = run_bass_kernel_spmd(nc, in_maps, core_ids=list(range(8)))
    if _raw:
        return res
    out = np.empty((4, 2048, 2048), dtype=np.float32)
    for c in range(8):
        b, h = c // 2, c % 2
        out[b, h * 1024:(h + 1) * 1024, :] = np.asarray(res.results[c]["outT"]).T
    return out
